# version 31
# baseline (speedup 1.0000x reference)
import math
from contextlib import ExitStack

import numpy as np
import ml_dtypes

import concourse.bass as bass
import concourse.mybir as mybir
from concourse.bass_utils import run_bass_kernel_spmd

F32 = mybir.dt.float32
BF16 = mybir.dt.bfloat16
I32 = mybir.dt.int32
I8 = mybir.dt.int8
AF = mybir.ActivationFunctionType
OP = mybir.AluOpType

NCORES = 8
D = 1024
H = 3072
T = 2048
TILE = 512
NT = T // TILE
DK = D // 128
RC = D // 128
HC = H // 128
LN32 = math.log(32.0)

IDENT_O = 0
ONES_O = 128
ABROW_O = 132
SROW_O = ABROW_O + 1024
SWROW_O = SROW_O + 1024
DUMPV_O = SWROW_O + 1024
X1TOK_O = DUMPV_O + 2048
X1TB_O = X1TOK_O + 4096
PROJT_O = X1TB_O + 8192
IWT_O = PROJT_O + 8192
WT_O = IWT_O + 8192
XFM_O = WT_O + 8192
XSQ_O = XFM_O + 8192
FEATS_O = XSQ_O + 4096
WGT_O = FEATS_O + 4096
WUT1_O = WGT_O + 24576
BF_END = WUT1_O + 18432
WDT_R = (PROJT_O, WT_O, XFM_O)

FXISC_O = 0
FBW_O = 8
FBF_O = 16
FEPS_O = 24
FC512_O = 25
FTMP_O = 26
FLN_O = 40
FOUT_O = 72
F32_END = FOUT_O + 2048

ENGINES = ("sync", "gpsimd", "vector", "scalar", "tensor")
SEMS = ("GS", "XR", "MM", "VE", "SE", "ST", "XSS", "XSL",
        "Lpj", "Lpj2", "Lab", "Liw", "Liw2", "Lxf0", "Lxf0b", "Lxf1", "Lwg", "Lwu", "Lwd")


class Sched:

    def __init__(self, sems):
        self.q = {e: [] for e in ENGINES}
        self.sems = sems
        self.c = {s: 0 for s in SEMS}
        self.lastwait = {}

    def raw(self, eng, fn):
        self.q[eng].append(fn)

    def wait(self, eng, sem, val):
        if val is None or val <= 0:
            return
        assert val <= self.c[sem], f"wait on future event {sem}={val} > {self.c[sem]}"
        key = (eng, sem)
        if self.lastwait.get(key, 0) >= val:
            return
        self.lastwait[key] = val
        h = self.sems[sem]
        self.q[eng].append(lambda e, h=h, v=val: e.wait_ge(h, v))

    def inc(self, eng, sem, fn, n=1):
        if (eng in ("vector", "scalar")) or (eng == "gpsimd" and sem == "GS"):
            self.wait(eng, sem, self.c[sem])
        h = self.sems[sem]
        self.c[sem] += n
        self.q[eng].append(lambda e, fn=fn, h=h, n=n: fn(e).then_inc(h, n))
        return self.c[sem]


def build_nc(stage=3, lnlv=5):
    nc = bass.Bass("TRN2", target_bir_lowering=False)

    xfm_h = nc.dram_tensor("xfm", [128, NT * 4096], F32, kind="ExternalInput")
    projT_h = nc.dram_tensor("projT", [128, 8192], F32, kind="ExternalInput")
    iwT_h = nc.dram_tensor("iwT", [128, 8192], F32, kind="ExternalInput")
    wgT_h = nc.dram_tensor("wgT", [128, 8 * H], I8, kind="ExternalInput")
    wuT_h = nc.dram_tensor("wuT", [128, 8 * H], I8, kind="ExternalInput")
    wdT_h = nc.dram_tensor("wdT", [128, 24 * D], I8, kind="ExternalInput")
    xisr_h = nc.dram_tensor("xisr", [1, D], F32, kind="ExternalInput")
    xisc_h = nc.dram_tensor("xisc", [128, DK], F32, kind="ExternalInput")
    out_h = nc.dram_tensor("out", [T, D], F32, kind="ExternalOutput")
    x1s_h = nc.dram_tensor("x1s", [128, NT * 4096], BF16)

    with ExitStack() as stack:
        e = stack.enter_context
        BF = e(nc.sbuf_tensor("bfarena", [128, BF_END], BF16))
        FA = e(nc.sbuf_tensor("f32arena", [128, F32_END], F32))
        P1 = e(nc.psum_tensor("P1", [128, 2048], F32))
        P2 = e(nc.psum_tensor("P2", [128, 1024], F32))
        PT3 = e(nc.psum_tensor("pt3", [128, 1024], F32))
        prow = PT3[0:1, 0:512]
        ptr = PT3[:, :].bitcast(BF16)[:, 1024:2048]
        pu = PT3[:, :]

        sems = {}
        for s in SEMS:
            sems[s] = e(nc.semaphore(s))

        em = Sched(sems)
        ev = {}

        def bf(a, b):
            return BF[:, a:a + b]

        def bfr(a, b):
            return BF[0:1, a:a + b]

        def fa(a, b):
            return FA[:, a:a + b]

        def pjap(rc, dk):
            return bf(PROJT_O + (rc // 4) * 4096 + dk * 512 + (rc % 4) * 128, 128)

        ident = bf(IDENT_O, 128)
        ones_col = bf(ONES_O, 1)
        abrow = bfr(ABROW_O, 1024)
        epsc = fa(FEPS_O, 1)
        c512 = fa(FC512_O, 1)
        xisc = fa(FXISC_O, DK)
        biasW = fa(FBW_O, DK)
        biasF = fa(FBF_O, DK)

        def lncol(m, j):
            return fa(FLN_O + 8 * (m % 4) + j, 1)

        def hact_ap(hc, lo, n):
            if hc < 8:
                base = XSQ_O + hc * 512
            elif hc < 16:
                base = FEATS_O + (hc - 8) * 512
            elif hc < 20:
                base = IWT_O + 6144 + (hc - 16) * 512
            else:
                base = X1TOK_O + (hc - 20) * 512
            return bf(base + lo, n)

        em.raw("sync", lambda e: e.dma_start(fa(FXISC_O, DK), xisc_h[:, :]).then_inc(sems["XR"], 16))
        em.c["XR"] += 16
        XR_ALL = em.c["XR"]

        ev["ve_ones1"] = em.inc(
            "vector", "VE", lambda e: e.memset(bf(DUMPV_O, 128), 1.0))

        def dma(eng, sem, dst, src):
            return em.inc(eng, sem, lambda e, d=dst, s=src: e.dma_start(d, s), 16)

        def xfsem(t):
            return "Lxf" + str(t % 2)

        def xfm_src(t):
            return xfm_h[:, t * 4096:(t + 1) * 4096]

        def xfm_dst(t):
            return bf(XFM_O + (t % 2) * 4096, 4096)

        ev["ld_xfm", 0] = dma("gpsimd", "Lxf0", bf(XFM_O, 2048), xfm_h[:, 0:2048])
        ev["ld_xfm0b"] = dma("gpsimd", "Lxf0b", bf(XFM_O + 2048, 2048),
                             xfm_h[:, 2048:4096])
        ev["ld_abrow"] = dma("gpsimd", "Lab", bfr(ABROW_O, 1024), xisr_h[:, :])
        ev["ld_projT"] = dma("gpsimd", "Lpj", bf(PROJT_O, 4096), projT_h[:, 0:4096])
        ev["ld_projT2"] = dma("gpsimd", "Lpj2", bf(PROJT_O + 4096, 4096),
                              projT_h[:, 4096:8192])
        ev["ld_iwT"] = dma("gpsimd", "Liw", bf(IWT_O, 4096), iwT_h[:, 0:4096])
        ev["ld_iwT2"] = dma("gpsimd", "Liw2", bf(IWT_O + 4096, 4096),
                            iwT_h[:, 4096:8192])
        ev["ld_xfm", 1] = dma("gpsimd", xfsem(1), xfm_dst(1), xfm_src(1))
        em.wait("gpsimd", "VE", ev["ve_ones1"])
        ev["gs_ident"] = em.inc(
            "gpsimd", "GS", lambda e: e.affine_select(
                ident, bf(DUMPV_O, 128), [[-1, 128]], OP.is_equal, 0.0,
                base=0, channel_multiplier=1))

        def emit_xsq(t):
            if t >= 1:
                em.wait("vector", "MM", ev["mm_ones", t - 1])
            for h in range(2):
                if t == 0:
                    em.wait("vector", "Lxf0" if h == 0 else "Lxf0b",
                            ev["ld_xfm", 0] if h == 0 else ev["ld_xfm0b"])
                else:
                    em.wait("vector", xfsem(t), ev["ld_xfm", t])
                ev["gs_xsq", t, h] = em.inc(
                    "vector", "VE",
                    lambda e, t=t, h=h: e.tensor_tensor(
                        bf(XSQ_O + h * 2048, 2048),
                        bf(XFM_O + (t % 2) * 4096 + h * 2048, 2048),
                        bf(XFM_O + (t % 2) * 4096 + h * 2048, 2048), OP.mult))

        def emit_iwsq(dk):
            em.wait("vector", "Liw" if dk < 4 else "Liw2",
                    ev["ld_iwT"] if dk < 4 else ev["ld_iwT2"])
            ev["gs_iwsq", dk] = em.inc(
                "vector", "VE",
                lambda e, dk=dk: e.tensor_tensor(
                    bf(X1TB_O + dk * 1024, 1024), bf(IWT_O + dk * 1024, 1024),
                    bf(IWT_O + dk * 1024, 1024), OP.mult))

        ftmp = fa(FTMP_O, DK)
        em.inc("vector", "VE", lambda e: e.memset(ones_col, 1.0))
        em.inc("vector", "VE", lambda e: e.memset(epsc, 1e-5))
        ev["ve_memsets"] = em.inc("vector", "VE", lambda e: e.memset(c512, 512.0))

        def emit_ones_mm(t):
            em.wait("tensor", "VE", ev["ve_memsets"])
            if t >= 1:
                em.wait("tensor", "SE", ev["se_srow", t - 1])
            for dk in range(DK):
                if dk % 4 == 0:
                    em.wait("tensor", "VE", ev["gs_xsq", t, dk // 4])
                def f(e, dk=dk, t=t):
                    return e.matmul(prow[:, :], ones_col,
                                    bf(XSQ_O + dk * 512, 512),
                                    start=(dk == 0), stop=(dk == DK - 1))
                if dk == DK - 1:
                    ev["mm_ones", t] = em.inc("tensor", "MM", f)
                else:
                    em.raw("tensor", lambda e, f=f: f(e))

        def emit_srow(t):
            em.wait("scalar", "VE", ev["ve_memsets"])
            em.wait("scalar", "MM", ev["mm_ones", t])
            if t >= 2:
                em.wait("scalar", "MM", ev["mm_dot", 8 * (t - 2) + RC - 1])
            ev["se_srow", t] = em.inc(
                "scalar", "SE",
                lambda e, t=t: e.activation(
                    bfr(SROW_O + (t % 2) * 512, 512), prow[:, :], AF.Identity,
                    bias=c512[0:1, :], scale=-0.5))

        def emit_mm1(t):
            em.wait("tensor", xfsem(t), ev["ld_xfm", t])
            if t == 0:
                em.wait("tensor", "Lxf0b", ev["ld_xfm0b"])
            em.wait("tensor", "Lab", ev["ld_abrow"])
            em.wait("tensor", "SE", ev["se_srow", t])
            xb = XFM_O + (t % 2) * 4096
            for rc in range(RC):
                k = 8 * t + rc
                if rc == 0:
                    em.wait("tensor", "Lpj", ev["ld_projT"])
                elif rc == 4:
                    em.wait("tensor", "Lpj2", ev["ld_projT2"])
                if k >= 2:
                    em.wait("tensor", "SE", ev["se_exp", k - 2])
                pslice = P2[:, (rc % 2) * 512:(rc % 2) * 512 + 512]
                for dk in range(DK):
                    em.raw("tensor", lambda e, rc=rc, dk=dk, xb=xb, ps=pslice: e.matmul(
                        ps, pjap(rc, dk),
                        bf(xb + dk * 512, 512), start=(dk == 0), stop=False))
                ev["mm_dot", k] = em.inc(
                    "tensor", "MM",
                    lambda e, rc=rc, t=t, ps=pslice: e.matmul(
                        ps, bfr(ABROW_O + rc * 128, 128),
                        bfr(SROW_O + (t % 2) * 512, 512), start=False, stop=True))
                em.wait("scalar", "VE", ev["ve_consts"])
                em.wait("scalar", "MM", ev["mm_dot", k])
                if rc == 0 and t >= 1:
                    em.wait("scalar", "MM", ev["mm_z1", 4 * (t - 1) + 3])
                ev["se_exp", k] = em.inc(
                    "scalar", "SE",
                    lambda e, rc=rc, ps=pslice: e.activation(
                        bf(FEATS_O + rc * 512, 512), ps, AF.Exp,
                        bias=biasF[:, rc:rc + 1], scale=xisc[:, rc:rc + 1]))

        def emit_swones(dk):
            em.wait("tensor", "VE", ev["gs_iwsq", dk])
            for oh in range(2):
                def f(e, dk=dk, oh=oh):
                    return e.matmul(P1[0:1, 1024 + oh * 512:1024 + oh * 512 + 512],
                                    ones_col,
                                    bf(X1TB_O + dk * 1024 + oh * 512, 512),
                                    start=(dk == 0), stop=(dk == DK - 1))
                if oh == 1:
                    ev["mm_swones", dk] = em.inc("tensor", "MM", f)
                else:
                    em.raw("tensor", lambda e, f=f: f(e))

        def emit_swrow():
            em.wait("scalar", "MM", ev["mm_swones", DK - 1])
            ev["se_swrow"] = em.inc(
                "scalar", "SE",
                lambda e: e.activation(bfr(SWROW_O, 1024), P1[0:1, 1024:2048],
                                       AF.Identity, scale=-0.5))

        def emit_dotw():
            em.wait("tensor", "SE", ev["se_swrow"])
            em.wait("tensor", "Liw", ev["ld_iwT"])
            em.wait("tensor", "Liw2", ev["ld_iwT2"])
            for rc in range(RC):
                if rc >= 2:
                    em.wait("tensor", "SE", ev["se_expW", rc - 2])
                base = (rc % 2) * 1024
                for dk in range(DK):
                    for oh in range(2):
                        em.raw("tensor", lambda e, rc=rc, dk=dk, oh=oh, base=base: e.matmul(
                            P1[:, base + oh * 512:base + oh * 512 + 512],
                            pjap(rc, dk),
                            bf(IWT_O + dk * 1024 + oh * 512, 512),
                            start=(dk == 0), stop=False))
                for oh in range(2):
                    def f(e, rc=rc, oh=oh, base=base):
                        return e.matmul(
                            P1[:, base + oh * 512:base + oh * 512 + 512],
                            bfr(ABROW_O + rc * 128, 128),
                            bfr(SWROW_O + oh * 512, 512), start=False, stop=True)
                    if oh == 1:
                        ev["mm_dotw", rc] = em.inc("tensor", "MM", f)
                    else:
                        em.raw("tensor", lambda e, f=f: f(e))
                em.wait("scalar", "MM", ev["mm_dotw", rc])
                em.wait("scalar", "VE", ev["ve_consts"])
                ev["se_expW", rc] = em.inc(
                    "scalar", "SE",
                    lambda e, rc=rc, base=base: e.activation(
                        bf(WT_O + rc * 1024, 1024), P1[:, base:base + 1024], AF.Exp,
                        bias=biasW[:, rc:rc + 1], scale=xisc[:, rc:rc + 1]))

        def ln_out_ap(i, phase):
            if phase == 0:
                return bf(X1TOK_O + (i % 4) * 1024, 1024)
            return fa(FOUT_O + (i % 2) * 1024, 1024)

        def emit_ln_head(i, ps, phase):
            m = i % 4
            negm = lncol(m, 0)
            s2 = lncol(m, 1)
            negS = lncol(m, 7)
            dv = bf(DUMPV_O + (i % 2) * 1024, 1024)
            out_ap = ln_out_ap(i, phase)
            mmev = ("mm_z1", i) if phase == 0 else ("mm_z2", i)
            em.wait("vector", "MM", ev[mmev])
            if phase == 0 and i >= 2:
                em.wait("vector", "SE", ev["se_apply", i - 2])
            elif phase == 1:
                if i >= 2:
                    em.wait("vector", "SE", ev["se_applyB", i - 2])
            tag_f = "psfree" if phase == 0 else "psfreeB"
            ev[tag_f, i] = em.inc("vector", "VE", lambda e, ps=ps, dv=dv, negS=negS:
                                  e.tensor_scalar(dv, ps, 1.0, 0.0, OP.mult, OP.add,
                                                  accum_out=negS))
            em.inc("vector", "VE", lambda e, negS=negS, negm=negm: e.tensor_scalar_mul(
                negm, negS, -1.0 / 1024.0))
            if phase == 0 and i >= 4 * NT - 4:
                em.wait("scalar", "VE", ev[tag_f, i])
                em.wait("scalar", "MM", ev["mm_tr", i - 4])
                em.inc("scalar", "SE", lambda e, dv=dv, s2=s2, out_ap=out_ap:
                       e.activation(out_ap, dv, AF.Square, accum_out=s2))
                return
            if phase == 0 and i >= 4:
                em.wait("vector", "MM", ev["mm_tr", i - 4])
            if phase == 1 and i >= 2:
                em.wait("vector", "ST", 16 * (i - 1))
            ev["ve_sq", i] = em.inc(
                "vector", "VE", lambda e, dv=dv, s2=s2, out_ap=out_ap:
                e.scalar_tensor_tensor(out_ap, dv, 1.0, dv, OP.mult, OP.mult,
                                       accum_out=s2))

        def emit_ln_tail(i, out_ap, phase):
            m = i % 4
            negm, s2, t2a, t2, lc, rc_, nmrs = (lncol(m, j) for j in range(7))
            dv = bf(DUMPV_O + (i % 2) * 1024, 1024)
            vv0 = em.inc("vector", "VE", lambda e, negm=negm, t2a=t2a: e.scalar_tensor_tensor(
                t2a, negm, -1024.0, negm, OP.mult, OP.mult))
            em.wait("scalar", "VE", vv0)
            em.inc("scalar", "SE", lambda e, s2=s2, t2a=t2a, t2=t2: e.activation(
                t2, s2, AF.Identity, bias=t2a))
            em.inc("scalar", "SE", lambda e, t2=t2, lc=lc: e.activation(
                lc, t2, AF.Ln, bias=epsc, scale=1.0 / 1024.0))
            em.inc("scalar", "SE", lambda e, lc=lc, rc_=rc_: e.activation(
                rc_, lc, AF.Exp, scale=-0.5))
            nmrs_ev = em.inc(
                "scalar", "SE", lambda e, negm=negm, rc_=rc_, nmrs=nmrs:
                e.activation(nmrs, negm, AF.Identity, scale=rc_[:, 0:1]))
            tag_a = "se_apply" if phase == 0 else "se_applyB"
            if phase == 0 and i >= 4 * NT - 2:
                ev["nmrs", i] = nmrs_ev
                ev["tailctx", i] = (dv, out_ap, rc_, nmrs)
                return
            if phase == 1 and i == 4 * NT - 1:
                ev["applyB_lo", i] = em.inc(
                    "scalar", "SE",
                    lambda e, dv=dv, out_ap=out_ap, rc_=rc_, nmrs=nmrs:
                    e.activation(out_ap[:, 0:512], dv[:, 0:512], AF.Identity,
                                 scale=rc_[:, 0:1], bias=nmrs))
                em.wait("vector", "SE", nmrs_ev)
                ev[tag_a, i] = em.inc(
                    "vector", "VE",
                    lambda e, dv=dv, out_ap=out_ap, rc_=rc_, nmrs=nmrs:
                    e.tensor_scalar(out_ap[:, 512:1024], dv[:, 512:1024],
                                    rc_, nmrs, OP.mult, OP.add))
                ev["applyv", i] = True
                return
            ev[tag_a, i] = em.inc(
                "scalar", "SE",
                lambda e, dv=dv, out_ap=out_ap, rc_=rc_, nmrs=nmrs: e.activation(
                    out_ap, dv, AF.Identity, scale=rc_[:, 0:1], bias=nmrs))

        def emit_apply_v(i):
            dv, out_ap, rc_, nmrs = ev["tailctx", i]
            ev["applyv", i] = True
            em.wait("vector", "SE", ev["nmrs", i])
            ev["se_apply", i] = em.inc(
                "vector", "VE",
                lambda e, dv=dv, out_ap=out_ap, rc_=rc_, nmrs=nmrs:
                e.tensor_scalar(out_ap, dv, rc_, nmrs, OP.mult, OP.add))

        def emit_ln(i, ps, out_ap, phase):
            emit_ln_head(i, ps, phase)
            emit_ln_tail(i, out_ap, phase)

        def emit_mm2(t):
            if t == 0:
                em.wait("tensor", "GS", ev["gs_ident"])
            em.wait("tensor", "SE", ev["se_expW", RC - 1])
            em.wait("tensor", "SE", ev["se_exp", 8 * t + RC - 1])
            xb = XFM_O + (t % 2) * 4096
            for m in range(4):
                i = 4 * t + m
                if t >= 1:
                    emit_tr(t - 1, m)
                if i >= 2:
                    em.wait("tensor", "VE", ev["psfree", i - 2])
                base = (m % 2) * 1024
                for rc in range(RC - 1):
                    for oh in range(2):
                        em.raw("tensor", lambda e, rc=rc, oh=oh, m=m, base=base: e.matmul(
                            P1[:, base + oh * 512:base + oh * 512 + 512],
                            bf(FEATS_O + rc * 512 + m * 128, 128),
                            bf(WT_O + rc * 1024 + oh * 512, 512),
                            start=(rc == 0), stop=False))
                for oc in range(DK):
                    em.raw("tensor", lambda e, oc=oc, m=m, base=base, xb=xb: e.matmul(
                        P1[:, base + oc * 128:base + oc * 128 + 128],
                        bf(xb + oc * 512 + m * 128, 128), ident,
                        start=False, stop=False))
                for oh in range(2):
                    def f(e, oh=oh, m=m, base=base):
                        return e.matmul(
                            P1[:, base + oh * 512:base + oh * 512 + 512],
                            bf(FEATS_O + (RC - 1) * 512 + m * 128, 128),
                            bf(WT_O + (RC - 1) * 1024 + oh * 512, 512),
                            start=False, stop=True)
                    if oh == 1:
                        ev["mm_z1", i] = em.inc("tensor", "MM", f)
                    else:
                        em.raw("tensor", lambda e, f=f: f(e))
                emit_ln_head(i, P1[:, base:base + 1024], phase=0)
                if m < 2 or t == NT - 1:
                    emit_ln_tail(i, bf(X1TOK_O + (i % 4) * 1024, 1024), phase=0)

        def emit_tr(t, m):
            i = 4 * t + m
            em.wait("tensor", "VE" if ev.get(("applyv", i)) else "SE",
                    ev["se_apply", i])
            if i >= 1:
                em.wait("tensor", "VE", ev["ve_evac", i - 1])
            for dk in range(DK):
                def f(e, dk=dk, i=i):
                    return e.transpose(ptr[:, dk * 128:dk * 128 + 128],
                                       bf(X1TOK_O + (i % 4) * 1024 + dk * 128, 128),
                                       ident)
                if dk == DK - 1:
                    ev["mm_tr", i] = em.inc("tensor", "MM", f)
                else:
                    em.raw("tensor", lambda e, f=f: f(e))
            em.wait("vector", "MM", ev["mm_tr", i])
            if i == 0:
                em.wait("vector", "MM", ev["mm_swones", DK - 1])
            if m == 0:
                em.wait("vector", "XSS", 16 * t)
            dst = BF[:, X1TB_O:X1TB_O + 4096].rearrange(
                "p (dk t) -> p dk t", dk=DK)[:, :, m * 128:m * 128 + 128]
            srcp = ptr[:, :].rearrange("p (dk t) -> p dk t", dk=DK)
            ev["ve_evac", i] = em.inc(
                "vector", "VE", lambda e, dst=dst, srcp=srcp: e.tensor_copy(dst, srcp))
            if m == 3:
                em.wait("sync", "VE", ev["ve_evac", i])
                em.wait("sync", "XSS", 16 * t)
                ev["xs_st", t] = em.inc(
                    "sync", "XSS",
                    lambda e, t=t: e.dma_start(
                        x1s_h[:, t * 4096:(t + 1) * 4096], bf(X1TB_O, 4096)), 16)
                if t == 0:
                    em.wait("sync", "MM", ev["mm_swones", DK - 1])
                    em.wait("sync", "XSS", ev["xs_st", 0])
                    ev["xs_ld", 0] = em.inc(
                        "sync", "XSL",
                        lambda e: e.dma_start(bf(X1TB_O + 4096, 4096),
                                              x1s_h[:, 0:4096]), 16)

        emit_xsq(0)
        em.wait("vector", "XR", XR_ALL)
        em.inc("vector", "VE", lambda e: e.tensor_scalar_add(biasW, xisc, -LN32))
        em.inc("vector", "VE", lambda e: e.tensor_mul(ftmp, xisc, xisc))
        ev["ve_consts"] = em.inc("vector", "VE", lambda e: e.scalar_tensor_tensor(
            biasF, ftmp, -512.0, biasW, OP.mult, OP.add))
        emit_ones_mm(0)
        emit_srow(0)
        emit_mm1(0)
        for dk in range(DK):
            emit_iwsq(dk)
            emit_swones(dk)
        emit_swrow()
        emit_dotw()
        ev["ld_wgT"] = dma("gpsimd", "Lwg", bf(WGT_O, 24576), wgT_h[:, :])
        emit_xsq(1)
        emit_ones_mm(1)
        emit_srow(1)
        emit_mm2(0)
        for t in range(1, NT):
            if t + 1 < NT:
                em.wait("gpsimd", "MM", ev["mm_z1", 4 * (t - 1) + 3])
                ev["ld_xfm", t + 1] = dma(
                    "gpsimd", xfsem(t + 1), xfm_dst(t + 1), xfm_src(t + 1))
                emit_xsq(t + 1)
            if t == 2:
                ev["ld_wuT1"] = dma("gpsimd", "Lwu", bf(WUT1_O, 18432),
                                    wuT_h[:, 0:18432])
                em.wait("gpsimd", "MM", ev["mm_dotw", RC - 1])
                ev["ld_wuT2"] = dma("gpsimd", "Lwu", bf(IWT_O, 6144),
                                    wuT_h[:, 18432:24576])
            emit_mm1(t)
            if t + 1 < NT:
                emit_ones_mm(t + 1)
                emit_srow(t + 1)
            for m in (2, 3):
                emit_ln_tail(4 * (t - 1) + m,
                             bf(X1TOK_O + ((4 * (t - 1) + m) % 4) * 1024, 1024), phase=0)
            emit_mm2(t)
        em.wait("gpsimd", "MM", ev["mm_z1", 4 * NT - 1])
        for j, reg in enumerate(WDT_R):
            ev["ld_wdT", j] = dma("gpsimd", "Lwd", bf(reg, 8192),
                                  wdT_h[:, 8192 * j:8192 * (j + 1)])

        def sg_ap(j):
            return bf(SWROW_O + (j % 2) * 512, 512)

        B_ORDER = [0, 1, 2, 3]
        B_BUF = [1, 0, 1, 0]

        def emit_lnB(pi_src, ms):
            t_src = B_ORDER[pi_src]
            for m in ms:
                i = 4 * pi_src + m
                emit_ln(i, P1[:, (m % 2) * 1024:(m % 2) * 1024 + 1024],
                        fa(FOUT_O + (i % 2) * 1024, 1024), phase=1)
                r0 = t_src * TILE + m * 128
                if i == 4 * NT - 1:
                    em.wait("sync", "SE", ev["applyB_lo", i])
                    em.wait("sync", "ST", 16 * i)
                    em.inc("sync", "ST",
                           lambda e, r0=r0, i=i: e.dma_start(
                               out_h[r0:r0 + 128, 0:512],
                               fa(FOUT_O + (i % 2) * 1024, 512)), 16)
                    em.wait("sync", "VE", ev["se_applyB", i])
                    em.inc("sync", "ST",
                           lambda e, r0=r0, i=i: e.dma_start(
                               out_h[r0:r0 + 128, 512:1024],
                               fa(FOUT_O + (i % 2) * 1024 + 512, 512)), 16)
                else:
                    em.wait("sync", "SE", ev["se_applyB", i])
                    em.wait("sync", "ST", 16 * i)
                    em.inc("sync", "ST",
                           lambda e, r0=r0, i=i: e.dma_start(
                               out_h[r0:r0 + 128, :],
                               fa(FOUT_O + (i % 2) * 1024, 1024)), 16)

        def emit_x1_prefetch(pi):
            if pi + 1 < NT and ("xs_ld", B_ORDER[pi + 1]) not in ev:
                tn = B_ORDER[pi + 1]
                em.wait("sync", "XSS", ev["xs_st", tn])
                if B_BUF[pi + 1] == 0:
                    em.wait("sync", "XSS", ev["xs_st", NT - 1])
                em.wait("sync", "XSL", em.c["XSL"])
                if pi >= 1:
                    em.wait("sync", "MM", ev["mm_z2", 4 * (pi - 1) + 3])
                ev["xs_ld", tn] = em.inc(
                    "sync", "XSL",
                    lambda e, tn=tn, pi=pi: e.dma_start(
                        bf(X1TB_O + B_BUF[pi + 1] * 4096, 4096),
                        x1s_h[:, tn * 4096:(tn + 1) * 4096]), 16)

        def bxb1(pi):
            return X1TB_O + B_BUF[pi] * 4096

        def emit_gate(pi, hc, mm_only=False):
            xb1 = bxb1(pi)
            x1_ev = ev.get(("xs_ld", B_ORDER[pi]))
            if True:
                j = HC * pi + hc
                if j == 0:
                    em.wait("tensor", "Lwg", ev["ld_wgT"])
                if hc == 0 and x1_ev is not None:
                    em.wait("tensor", "XSL", x1_ev)
                if j >= 2:
                    em.wait("tensor", "SE", ev["se_silu", j - 2])
                pg = P2[:, (j % 2) * 512:(j % 2) * 512 + 512]
                for dk in range(DK):
                    def f(e, dk=dk, hc=hc, xb1=xb1, pg=pg):
                        return e.matmul(
                            pg, bf(WGT_O + dk * 3072 + hc * 128, 128),
                            bf(xb1 + dk * 512, 512),
                            start=(dk == 0), stop=(dk == DK - 1))
                    if dk == DK - 1:
                        ev["mm_g", j] = em.inc("tensor", "MM", f)
                    else:
                        em.raw("tensor", lambda e, f=f: f(e))
                if not mm_only:
                    emit_silu(pi, hc)

        def emit_silu(pi, hc):
            if True:
                j = HC * pi + hc
                pg = P2[:, (j % 2) * 512:(j % 2) * 512 + 512]
                em.wait("scalar", "MM", ev["mm_g", j])
                if j >= 2:
                    em.wait("scalar", "VE", ev["ve_h", j - 2])
                ev["se_silu", j] = em.inc(
                    "scalar", "SE", lambda e, j=j, pg=pg: e.activation(
                        sg_ap(j), pg, AF.Silu, scale=1.0 / 127.0))

        def emit_up(pi, hc):
            xb1 = bxb1(pi)
            if True:
                j = HC * pi + hc
                if j == 0:
                    em.wait("tensor", "Lwu", ev["ld_wuT2"])
                if pi == 0 and hc == 0:
                    em.wait("tensor", "SE", ev["se_srow", NT - 1])
                if pi == 0 and hc == 1:
                    em.wait("tensor", "VE", ev["ve_evac", 4 * NT - 1])
                if j >= 2:
                    em.wait("tensor", "VE", ev["ve_h", j - 2])
                puj = pu[:, (j % 2) * 512:(j % 2) * 512 + 512]
                for dk in range(DK):
                    wu_off = (WUT1_O + dk * 3072) if dk < 6 else (IWT_O + (dk - 6) * 3072)
                    def f(e, dk=dk, hc=hc, xb1=xb1, puj=puj, wu_off=wu_off):
                        return e.matmul(
                            puj, bf(wu_off + hc * 128, 128),
                            bf(xb1 + dk * 512, 512),
                            start=(dk == 0), stop=(dk == DK - 1))
                    if dk == DK - 1:
                        ev["mm_u", j] = em.inc("tensor", "MM", f)
                    else:
                        em.raw("tensor", lambda e, f=f: f(e))
                em.wait("vector", "MM", ev["mm_u", j])
                em.wait("vector", "SE", ev["se_silu", j])
                if hc == 0 and pi >= 1:
                    em.wait("vector", "MM", ev["mm_z2", 4 * (pi - 1) + 3])
                ev["ve_h", j] = em.inc(
                    "vector", "VE", lambda e, j=j, hc=hc, puj=puj: e.scalar_tensor_tensor(
                        hact_ap(hc, 0, 512), puj, 1.0 / 16129.0, sg_ap(j),
                        OP.mult, OP.mult))

        for pi, t in enumerate(B_ORDER):
            xb1 = bxb1(pi)
            if pi >= 1:
                emit_x1_prefetch(pi)
            if pi == 0:
                for hc in range(4):
                    if hc >= 2:
                        emit_apply_v(4 * (NT - 1) + hc)
                    emit_tr(NT - 1, hc)
                    emit_gate(pi, hc, mm_only=True)
                    if hc < 2:
                        emit_silu(pi, hc)
                for hc in range(4):
                    emit_up(pi, hc)
                    if hc < 2:
                        emit_silu(pi, hc + 2)
                for hc in range(4, HC):
                    emit_gate(pi, hc)
                    emit_up(pi, hc)
            else:
                for hc in range(2, HC):
                    emit_up(pi, hc - 2)
                    emit_gate(pi, hc)
                emit_up(pi, HC - 2)
                emit_up(pi, HC - 1)
            if pi == 0:
                emit_x1_prefetch(pi)
            if pi == 0:
                em.wait("tensor", "Lwd", ev["ld_wdT", 2])
            for m in range(4):
                i = 4 * pi + m
                if i >= 2:
                    em.wait("tensor", "VE", ev["psfreeB", i - 2])
                base = (m % 2) * 1024
                for hc in range(HC - 1):
                    if m == 0:
                        em.wait("tensor", "VE", ev["ve_h", HC * pi + hc])
                    reg = WDT_R[hc // 8]
                    for oh in range(2):
                        em.raw("tensor", lambda e, hc=hc, oh=oh, m=m, base=base, reg=reg:
                               e.matmul(
                                   P1[:, base + oh * 512:base + oh * 512 + 512],
                                   hact_ap(hc, m * 128, 128),
                                   bf(reg + (hc % 8) * 1024 + oh * 512, 512),
                                   start=(hc == 0), stop=False))
                for oc in range(DK):
                    em.raw("tensor", lambda e, oc=oc, m=m, base=base, xb1=xb1: e.matmul(
                        P1[:, base + oc * 128:base + oc * 128 + 128],
                        bf(xb1 + oc * 512 + m * 128, 128), ident,
                        start=False, stop=False))
                hcl = HC - 1
                regl = WDT_R[hcl // 8]
                if m == 0:
                    em.wait("tensor", "VE", ev["ve_h", HC * pi + hcl])
                for oh in range(2):
                    def f(e, oh=oh, m=m, base=base, hcl=hcl, regl=regl):
                        return e.matmul(
                            P1[:, base + oh * 512:base + oh * 512 + 512],
                            hact_ap(hcl, m * 128, 128),
                            bf(regl + (hcl % 8) * 1024 + oh * 512, 512),
                            start=False, stop=True)
                    if oh == 1:
                        ev["mm_z2", i] = em.inc("tensor", "MM", f)
                    else:
                        em.raw("tensor", lambda e, f=f: f(e))
                emit_lnB(pi, (m,))
                if m < 2 and pi + 1 < NT:
                    emit_gate(pi + 1, m)

        em.wait("sync", "ST", 16 * (4 * NT + 1))

        with nc.Block() as block:
            @block.sync
            def _(e):
                for f in em.q["sync"]:
                    f(e)

            @block.gpsimd
            def _(e):
                for f in em.q["gpsimd"]:
                    f(e)

            @block.vector
            def _(e):
                for f in em.q["vector"]:
                    f(e)

            @block.scalar
            def _(e):
                for f in em.q["scalar"]:
                    f(e)

            @block.tensor
            def _(e):
                for f in em.q["tensor"]:
                    f(e)

    return nc


_NC_CACHE = None


def _get_nc():
    global _NC_CACHE
    if _NC_CACHE is None:
        _NC_CACHE = build_nc()
    return _NC_CACHE


def kernel(L0_attn_x, L0_attn_input_weights, L0_attn_xis, L0_attn_proj_matrix,
           L0_ffn_w_gate, L0_ffn_w_up, L0_ffn_w_down):
    x = np.asarray(L0_attn_x, dtype=np.float32)
    iw = np.asarray(L0_attn_input_weights, dtype=np.float32)
    xis = np.asarray(L0_attn_xis, dtype=np.float32)
    proj = np.asarray(L0_attn_proj_matrix, dtype=np.float32)
    wg = np.asarray(L0_ffn_w_gate, dtype=np.int32)
    wu = np.asarray(L0_ffn_w_up, dtype=np.int32)
    wd = np.asarray(L0_ffn_w_down, dtype=np.int32)

    def pack(mT, nchunk):
        f = mT.shape[1]
        return np.ascontiguousarray(
            mT.reshape(nchunk, 128, f).transpose(1, 0, 2).reshape(128, nchunk * f))

    xr = x.reshape(16384, D)
    xfmT = xr.T
    projT = np.ascontiguousarray(
        proj.T.reshape(DK, 128, 2, 512).transpose(1, 2, 0, 3).reshape(128, 8192))
    iwT = pack(np.ascontiguousarray(iw.T), DK)
    wgT = pack(np.ascontiguousarray(wg.T.astype(np.int8)), DK)
    wuT = pack(np.ascontiguousarray(wu.T.astype(np.int8)), DK)
    wdT = pack(np.ascontiguousarray(wd.T.astype(np.int8)), HC)
    xisr = np.ascontiguousarray(xis.reshape(1, D))
    xisc = np.ascontiguousarray(xis.reshape(DK, 128).T)

    in_maps = []
    for c in range(NCORES):
        xfc = xfmT[:, c * T:(c + 1) * T].reshape(DK, 128, NT, TILE)
        xfc = np.ascontiguousarray(
            xfc.transpose(1, 2, 0, 3).reshape(128, NT * 4096))
        in_maps.append(dict(
            xfm=xfc,
            projT=projT, iwT=iwT, wgT=wgT, wuT=wuT, wdT=wdT,
            xisr=xisr, xisc=xisc))

    nc = _get_nc()
    res = run_bass_kernel_spmd(nc, in_maps, core_ids=list(range(NCORES)))
    out = np.concatenate([res.results[c]["out"] for c in range(NCORES)], axis=0)
    return out.reshape(2, 8192, D)


# version 32
# speedup vs baseline: 1.0092x; 1.0092x over previous
import math
from contextlib import ExitStack

import numpy as np
import ml_dtypes

import concourse.bass as bass
import concourse.mybir as mybir
from concourse.bass_utils import run_bass_kernel_spmd

F32 = mybir.dt.float32
BF16 = mybir.dt.bfloat16
I32 = mybir.dt.int32
I8 = mybir.dt.int8
AF = mybir.ActivationFunctionType
OP = mybir.AluOpType

NCORES = 8
D = 1024
H = 3072
T = 2048
TILE = 512
NT = T // TILE
DK = D // 128
RC = D // 128
HC = H // 128
LN32 = math.log(32.0)

IDENT_O = 0
ONES_O = 128
ABROW_O = 132
SROW_O = ABROW_O + 1024
SWROW_O = SROW_O + 1024
DUMPV_O = SWROW_O + 1024
X1TOK_O = DUMPV_O + 2048
X1TB_O = X1TOK_O + 4096
PROJT_O = X1TB_O + 8192
IWT_O = PROJT_O + 8192
WT_O = IWT_O + 8192
XFM_O = WT_O + 8192
XSQ_O = XFM_O + 8192
FEATS_O = XSQ_O + 4096
WGT_O = FEATS_O + 4096
WUT1_O = WGT_O + 24576
BF_END = WUT1_O + 18432
WDT_R = (PROJT_O, WT_O, XFM_O)

FXISC_O = 0
FBW_O = 8
FBF_O = 16
FEPS_O = 24
FC512_O = 25
FTMP_O = 26
FLN_O = 40
FOUT_O = 72
F32_END = FOUT_O + 2048

ENGINES = ("sync", "gpsimd", "vector", "scalar", "tensor")
SEMS = ("GS", "XR", "MM", "VE", "SE", "ST", "XSS", "XSL",
        "Lpj", "Lpj2", "Lab", "Liw", "Liw2", "Lxf0", "Lxf0b", "Lxf1", "Lwg", "Lwu", "Lwd",
        "XTS", "XTL")


class Sched:

    def __init__(self, sems):
        self.q = {e: [] for e in ENGINES}
        self.sems = sems
        self.c = {s: 0 for s in SEMS}
        self.lastwait = {}

    def raw(self, eng, fn):
        self.q[eng].append(fn)

    def wait(self, eng, sem, val):
        if val is None or val <= 0:
            return
        assert val <= self.c[sem], f"wait on future event {sem}={val} > {self.c[sem]}"
        key = (eng, sem)
        if self.lastwait.get(key, 0) >= val:
            return
        self.lastwait[key] = val
        h = self.sems[sem]
        self.q[eng].append(lambda e, h=h, v=val: e.wait_ge(h, v))

    def inc(self, eng, sem, fn, n=1):
        if (eng in ("vector", "scalar")) or (eng == "gpsimd" and sem == "GS"):
            self.wait(eng, sem, self.c[sem])
        h = self.sems[sem]
        self.c[sem] += n
        self.q[eng].append(lambda e, fn=fn, h=h, n=n: fn(e).then_inc(h, n))
        return self.c[sem]


def build_nc(stage=3, lnlv=5):
    nc = bass.Bass("TRN2", target_bir_lowering=False)

    xfm_h = nc.dram_tensor("xfm", [128, NT * 4096], F32, kind="ExternalInput")
    projT_h = nc.dram_tensor("projT", [128, 8192], F32, kind="ExternalInput")
    iwT_h = nc.dram_tensor("iwT", [128, 8192], F32, kind="ExternalInput")
    wgT_h = nc.dram_tensor("wgT", [128, 8 * H], I8, kind="ExternalInput")
    wuT_h = nc.dram_tensor("wuT", [128, 8 * H], I8, kind="ExternalInput")
    wdT_h = nc.dram_tensor("wdT", [128, 24 * D], I8, kind="ExternalInput")
    xisr_h = nc.dram_tensor("xisr", [1, D], F32, kind="ExternalInput")
    xisc_h = nc.dram_tensor("xisc", [128, DK], F32, kind="ExternalInput")
    out_h = nc.dram_tensor("out", [T, D], F32, kind="ExternalOutput")
    x1s_h = nc.dram_tensor("x1s", [128, NT * 4096], BF16)
    x1t_h = nc.dram_tensor("x1t", [T, D], BF16)

    with ExitStack() as stack:
        e = stack.enter_context
        BF = e(nc.sbuf_tensor("bfarena", [128, BF_END], BF16))
        FA = e(nc.sbuf_tensor("f32arena", [128, F32_END], F32))
        P1 = e(nc.psum_tensor("P1", [128, 2048], F32))
        P2 = e(nc.psum_tensor("P2", [128, 1024], F32))
        PT3 = e(nc.psum_tensor("pt3", [128, 1024], F32))
        prow = PT3[0:1, 0:512]
        ptr = PT3[:, :].bitcast(BF16)[:, 1024:2048]
        pu = PT3[:, :]

        sems = {}
        for s in SEMS:
            sems[s] = e(nc.semaphore(s))

        em = Sched(sems)
        ev = {}

        def bf(a, b):
            return BF[:, a:a + b]

        def bfr(a, b):
            return BF[0:1, a:a + b]

        def fa(a, b):
            return FA[:, a:a + b]

        def pjap(rc, dk):
            return bf(PROJT_O + (rc // 4) * 4096 + dk * 512 + (rc % 4) * 128, 128)

        ident = bf(IDENT_O, 128)
        ones_col = bf(ONES_O, 1)
        abrow = bfr(ABROW_O, 1024)
        epsc = fa(FEPS_O, 1)
        c512 = fa(FC512_O, 1)
        xisc = fa(FXISC_O, DK)
        biasW = fa(FBW_O, DK)
        biasF = fa(FBF_O, DK)

        def xres_ap(i):
            return bf(ABROW_O + (i % 2) * 1024, 1024)

        def lncol(m, j):
            return fa(FLN_O + 8 * (m % 4) + j, 1)

        def hact_ap(hc, lo, n):
            if hc < 8:
                base = XSQ_O + hc * 512
            elif hc < 16:
                base = FEATS_O + (hc - 8) * 512
            elif hc < 20:
                base = IWT_O + 6144 + (hc - 16) * 512
            else:
                base = X1TOK_O + (hc - 20) * 512
            return bf(base + lo, n)

        em.raw("sync", lambda e: e.dma_start(fa(FXISC_O, DK), xisc_h[:, :]).then_inc(sems["XR"], 16))
        em.c["XR"] += 16
        XR_ALL = em.c["XR"]

        ev["ve_ones1"] = em.inc(
            "vector", "VE", lambda e: e.memset(bf(DUMPV_O, 128), 1.0))

        def dma(eng, sem, dst, src):
            return em.inc(eng, sem, lambda e, d=dst, s=src: e.dma_start(d, s), 16)

        def xfsem(t):
            return "Lxf" + str(t % 2)

        def xfm_src(t):
            return xfm_h[:, t * 4096:(t + 1) * 4096]

        def xfm_dst(t):
            return bf(XFM_O + (t % 2) * 4096, 4096)

        ev["ld_xfm", 0] = dma("gpsimd", "Lxf0", bf(XFM_O, 2048), xfm_h[:, 0:2048])
        ev["ld_xfm0b"] = dma("gpsimd", "Lxf0b", bf(XFM_O + 2048, 2048),
                             xfm_h[:, 2048:4096])
        ev["ld_abrow"] = dma("gpsimd", "Lab", bfr(ABROW_O, 1024), xisr_h[:, :])
        ev["ld_projT"] = dma("gpsimd", "Lpj", bf(PROJT_O, 4096), projT_h[:, 0:4096])
        ev["ld_projT2"] = dma("gpsimd", "Lpj2", bf(PROJT_O + 4096, 4096),
                              projT_h[:, 4096:8192])
        ev["ld_iwT"] = dma("gpsimd", "Liw", bf(IWT_O, 4096), iwT_h[:, 0:4096])
        ev["ld_iwT2"] = dma("gpsimd", "Liw2", bf(IWT_O + 4096, 4096),
                            iwT_h[:, 4096:8192])
        ev["ld_xfm", 1] = dma("gpsimd", xfsem(1), xfm_dst(1), xfm_src(1))
        em.wait("gpsimd", "VE", ev["ve_ones1"])
        ev["gs_ident"] = em.inc(
            "gpsimd", "GS", lambda e: e.affine_select(
                ident, bf(DUMPV_O, 128), [[-1, 128]], OP.is_equal, 0.0,
                base=0, channel_multiplier=1))

        def emit_xsq(t):
            if t >= 1:
                em.wait("vector", "MM", ev["mm_ones", t - 1])
            for h in range(2):
                if t == 0:
                    em.wait("vector", "Lxf0" if h == 0 else "Lxf0b",
                            ev["ld_xfm", 0] if h == 0 else ev["ld_xfm0b"])
                else:
                    em.wait("vector", xfsem(t), ev["ld_xfm", t])
                ev["gs_xsq", t, h] = em.inc(
                    "vector", "VE",
                    lambda e, t=t, h=h: e.tensor_tensor(
                        bf(XSQ_O + h * 2048, 2048),
                        bf(XFM_O + (t % 2) * 4096 + h * 2048, 2048),
                        bf(XFM_O + (t % 2) * 4096 + h * 2048, 2048), OP.mult))

        def emit_iwsq(dk):
            em.wait("vector", "Liw" if dk < 4 else "Liw2",
                    ev["ld_iwT"] if dk < 4 else ev["ld_iwT2"])
            ev["gs_iwsq", dk] = em.inc(
                "vector", "VE",
                lambda e, dk=dk: e.tensor_tensor(
                    bf(X1TB_O + dk * 1024, 1024), bf(IWT_O + dk * 1024, 1024),
                    bf(IWT_O + dk * 1024, 1024), OP.mult))

        ftmp = fa(FTMP_O, DK)
        em.wait("vector", "XR", XR_ALL)
        em.inc("vector", "VE", lambda e: e.memset(ones_col, 1.0))
        em.inc("vector", "VE", lambda e: e.memset(epsc, 1e-5))
        em.inc("vector", "VE", lambda e: e.memset(c512, 512.0))
        em.inc("vector", "VE", lambda e: e.tensor_scalar_add(biasW, xisc, -LN32))
        em.inc("vector", "VE", lambda e: e.tensor_mul(ftmp, xisc, xisc))
        ev["ve_consts"] = em.inc("vector", "VE", lambda e: e.scalar_tensor_tensor(
            biasF, ftmp, -512.0, biasW, OP.mult, OP.add))

        def emit_ones_mm(t):
            em.wait("tensor", "VE", ev["ve_consts"])
            if t >= 1:
                em.wait("tensor", "SE", ev["se_srow", t - 1])
            for dk in range(DK):
                if dk % 4 == 0:
                    em.wait("tensor", "VE", ev["gs_xsq", t, dk // 4])
                def f(e, dk=dk, t=t):
                    return e.matmul(prow[:, :], ones_col,
                                    bf(XSQ_O + dk * 512, 512),
                                    start=(dk == 0), stop=(dk == DK - 1))
                if dk == DK - 1:
                    ev["mm_ones", t] = em.inc("tensor", "MM", f)
                else:
                    em.raw("tensor", lambda e, f=f: f(e))

        def emit_srow(t):
            em.wait("scalar", "MM", ev["mm_ones", t])
            if t >= 2:
                em.wait("scalar", "MM", ev["mm_dot", 8 * (t - 2) + RC - 1])
            ev["se_srow", t] = em.inc(
                "scalar", "SE",
                lambda e, t=t: e.activation(
                    bfr(SROW_O + (t % 2) * 512, 512), prow[:, :], AF.Identity,
                    bias=c512[0:1, :], scale=-0.5))

        def emit_mm1(t):
            em.wait("tensor", xfsem(t), ev["ld_xfm", t])
            if t == 0:
                em.wait("tensor", "Lxf0b", ev["ld_xfm0b"])
            em.wait("tensor", "Lab", ev["ld_abrow"])
            em.wait("tensor", "SE", ev["se_srow", t])
            xb = XFM_O + (t % 2) * 4096
            for rc in range(RC):
                k = 8 * t + rc
                if rc == 0:
                    em.wait("tensor", "Lpj", ev["ld_projT"])
                elif rc == 4:
                    em.wait("tensor", "Lpj2", ev["ld_projT2"])
                if k >= 2:
                    em.wait("tensor", "SE", ev["se_exp", k - 2])
                pslice = P2[:, (rc % 2) * 512:(rc % 2) * 512 + 512]
                for dk in range(DK):
                    em.raw("tensor", lambda e, rc=rc, dk=dk, xb=xb, ps=pslice: e.matmul(
                        ps, pjap(rc, dk),
                        bf(xb + dk * 512, 512), start=(dk == 0), stop=False))
                ev["mm_dot", k] = em.inc(
                    "tensor", "MM",
                    lambda e, rc=rc, t=t, ps=pslice: e.matmul(
                        ps, bfr(ABROW_O + rc * 128, 128),
                        bfr(SROW_O + (t % 2) * 512, 512), start=False, stop=True))
                em.wait("scalar", "MM", ev["mm_dot", k])
                if rc == 0 and t >= 1:
                    em.wait("scalar", "MM", ev["mm_z1", 4 * (t - 1) + 3])
                ev["se_exp", k] = em.inc(
                    "scalar", "SE",
                    lambda e, rc=rc, ps=pslice: e.activation(
                        bf(FEATS_O + rc * 512, 512), ps, AF.Exp,
                        bias=biasF[:, rc:rc + 1], scale=xisc[:, rc:rc + 1]))

        def emit_swones(dk):
            em.wait("tensor", "VE", ev["gs_iwsq", dk])
            for oh in range(2):
                def f(e, dk=dk, oh=oh):
                    return e.matmul(P1[0:1, 1024 + oh * 512:1024 + oh * 512 + 512],
                                    ones_col,
                                    bf(X1TB_O + dk * 1024 + oh * 512, 512),
                                    start=(dk == 0), stop=(dk == DK - 1))
                if oh == 1:
                    ev["mm_swones", dk] = em.inc("tensor", "MM", f)
                else:
                    em.raw("tensor", lambda e, f=f: f(e))

        def emit_swrow():
            em.wait("scalar", "MM", ev["mm_swones", DK - 1])
            ev["se_swrow"] = em.inc(
                "scalar", "SE",
                lambda e: e.activation(bfr(SWROW_O, 1024), P1[0:1, 1024:2048],
                                       AF.Identity, scale=-0.5))

        def emit_dotw():
            em.wait("tensor", "SE", ev["se_swrow"])
            em.wait("tensor", "Liw", ev["ld_iwT"])
            em.wait("tensor", "Liw2", ev["ld_iwT2"])
            for rc in range(RC):
                if rc >= 2:
                    em.wait("tensor", "SE", ev["se_expW", rc - 2])
                base = (rc % 2) * 1024
                for dk in range(DK):
                    for oh in range(2):
                        em.raw("tensor", lambda e, rc=rc, dk=dk, oh=oh, base=base: e.matmul(
                            P1[:, base + oh * 512:base + oh * 512 + 512],
                            pjap(rc, dk),
                            bf(IWT_O + dk * 1024 + oh * 512, 512),
                            start=(dk == 0), stop=False))
                for oh in range(2):
                    def f(e, rc=rc, oh=oh, base=base):
                        return e.matmul(
                            P1[:, base + oh * 512:base + oh * 512 + 512],
                            bfr(ABROW_O + rc * 128, 128),
                            bfr(SWROW_O + oh * 512, 512), start=False, stop=True)
                    if oh == 1:
                        ev["mm_dotw", rc] = em.inc("tensor", "MM", f)
                    else:
                        em.raw("tensor", lambda e, f=f: f(e))
                em.wait("scalar", "MM", ev["mm_dotw", rc])
                em.wait("scalar", "VE", ev["ve_consts"])
                ev["se_expW", rc] = em.inc(
                    "scalar", "SE",
                    lambda e, rc=rc, base=base: e.activation(
                        bf(WT_O + rc * 1024, 1024), P1[:, base:base + 1024], AF.Exp,
                        bias=biasW[:, rc:rc + 1], scale=xisc[:, rc:rc + 1]))

        def ln_out_ap(i, phase):
            if phase == 0:
                return bf(X1TOK_O + (i % 4) * 1024, 1024)
            return fa(FOUT_O + (i % 2) * 1024, 1024)

        def emit_ln_head(i, ps, phase):
            m = i % 4
            negm = lncol(m, 0)
            s2 = lncol(m, 1)
            negS = lncol(m, 7)
            dv = bf(DUMPV_O + (i % 2) * 1024, 1024)
            out_ap = ln_out_ap(i, phase)
            mmev = ("mm_z1", i) if phase == 0 else ("mm_z2", i)
            em.wait("vector", "MM", ev[mmev])
            if phase == 0 and i >= 2:
                em.wait("vector", "SE", ev["se_apply", i - 2])
            elif phase == 1:
                if i >= 2:
                    em.wait("vector", "SE", ev["se_applyB", i - 2])
            tag_f = "psfree" if phase == 0 else "psfreeB"
            if phase == 1:
                em.wait("vector", "XTL", ev["x1t_ld", i])
                ev[tag_f, i] = em.inc(
                    "vector", "VE",
                    lambda e, ps=ps, dv=dv, negS=negS, xr=xres_ap(i):
                    e.scalar_tensor_tensor(dv, ps, 1.0, xr, OP.mult, OP.add,
                                           accum_out=negS))
            else:
                ev[tag_f, i] = em.inc(
                    "vector", "VE", lambda e, ps=ps, dv=dv, negS=negS:
                    e.tensor_scalar(dv, ps, 1.0, 0.0, OP.mult, OP.add,
                                    accum_out=negS))
            em.inc("vector", "VE", lambda e, negS=negS, negm=negm: e.tensor_scalar_mul(
                negm, negS, -1.0 / 1024.0))
            if phase == 0 and i >= 4 * NT - 4:
                em.wait("scalar", "VE", ev[tag_f, i])
                em.wait("scalar", "MM", ev["mm_tr", i - 4])
                em.wait("scalar", "XTS", ev["x1t_st", i - 4])
                em.inc("scalar", "SE", lambda e, dv=dv, s2=s2, out_ap=out_ap:
                       e.activation(out_ap, dv, AF.Square, accum_out=s2))
                return
            if phase == 0 and i >= 4:
                em.wait("vector", "MM", ev["mm_tr", i - 4])
                em.wait("vector", "XTS", ev["x1t_st", i - 4])
            if phase == 1 and i >= 2:
                em.wait("vector", "ST", 16 * (i - 1))
            ev["ve_sq", i] = em.inc(
                "vector", "VE", lambda e, dv=dv, s2=s2, out_ap=out_ap:
                e.scalar_tensor_tensor(out_ap, dv, 1.0, dv, OP.mult, OP.mult,
                                       accum_out=s2))

        def emit_ln_tail(i, out_ap, phase):
            m = i % 4
            negm, s2, t2a, t2, lc, rc_, nmrs = (lncol(m, j) for j in range(7))
            dv = bf(DUMPV_O + (i % 2) * 1024, 1024)
            vv0 = em.inc("vector", "VE", lambda e, negm=negm, t2a=t2a: e.scalar_tensor_tensor(
                t2a, negm, -1024.0, negm, OP.mult, OP.mult))
            em.wait("scalar", "VE", vv0)
            em.inc("scalar", "SE", lambda e, s2=s2, t2a=t2a, t2=t2: e.activation(
                t2, s2, AF.Identity, bias=t2a))
            em.inc("scalar", "SE", lambda e, t2=t2, lc=lc: e.activation(
                lc, t2, AF.Ln, bias=epsc, scale=1.0 / 1024.0))
            em.inc("scalar", "SE", lambda e, lc=lc, rc_=rc_: e.activation(
                rc_, lc, AF.Exp, scale=-0.5))
            nmrs_ev = em.inc(
                "scalar", "SE", lambda e, negm=negm, rc_=rc_, nmrs=nmrs:
                e.activation(nmrs, negm, AF.Identity, scale=rc_[:, 0:1]))
            tag_a = "se_apply" if phase == 0 else "se_applyB"
            if phase == 0 and i >= 4 * NT - 2:
                ev["nmrs", i] = nmrs_ev
                ev["tailctx", i] = (dv, out_ap, rc_, nmrs)
                return
            if phase == 1 and i == 4 * NT - 1:
                ev["applyB_lo", i] = em.inc(
                    "scalar", "SE",
                    lambda e, dv=dv, out_ap=out_ap, rc_=rc_, nmrs=nmrs:
                    e.activation(out_ap[:, 0:512], dv[:, 0:512], AF.Identity,
                                 scale=rc_[:, 0:1], bias=nmrs))
                em.wait("vector", "SE", nmrs_ev)
                ev[tag_a, i] = em.inc(
                    "vector", "VE",
                    lambda e, dv=dv, out_ap=out_ap, rc_=rc_, nmrs=nmrs:
                    e.tensor_scalar(out_ap[:, 512:1024], dv[:, 512:1024],
                                    rc_, nmrs, OP.mult, OP.add))
                ev["applyv", i] = True
                return
            ev[tag_a, i] = em.inc(
                "scalar", "SE",
                lambda e, dv=dv, out_ap=out_ap, rc_=rc_, nmrs=nmrs: e.activation(
                    out_ap, dv, AF.Identity, scale=rc_[:, 0:1], bias=nmrs))
            if phase == 0:
                emit_x1t_store(i, out_ap, "SE")

        def emit_x1t_store(i, out_ap, eng):
            em.wait("sync", eng, ev["se_apply", i])
            em.wait("sync", "XTS", 16 * i)
            ev["x1t_st", i] = em.inc(
                "sync", "XTS",
                lambda e, i=i, out_ap=out_ap: e.dma_start(
                    x1t_h[i * 128:(i + 1) * 128, :], out_ap), 16)

        def emit_apply_v(i):
            dv, out_ap, rc_, nmrs = ev["tailctx", i]
            ev["applyv", i] = True
            em.wait("vector", "SE", ev["nmrs", i])
            ev["se_apply", i] = em.inc(
                "vector", "VE",
                lambda e, dv=dv, out_ap=out_ap, rc_=rc_, nmrs=nmrs:
                e.tensor_scalar(out_ap, dv, rc_, nmrs, OP.mult, OP.add))
            emit_x1t_store(i, out_ap, "VE")

        def emit_ln(i, ps, out_ap, phase):
            emit_ln_head(i, ps, phase)
            emit_ln_tail(i, out_ap, phase)

        def emit_mm2(t):
            if t == 0:
                em.wait("tensor", "GS", ev["gs_ident"])
            em.wait("tensor", "SE", ev["se_expW", RC - 1])
            em.wait("tensor", "SE", ev["se_exp", 8 * t + RC - 1])
            xb = XFM_O + (t % 2) * 4096
            for m in range(4):
                i = 4 * t + m
                if t >= 1:
                    emit_tr(t - 1, m)
                if i >= 2:
                    em.wait("tensor", "VE", ev["psfree", i - 2])
                base = (m % 2) * 1024
                for rc in range(RC - 1):
                    for oh in range(2):
                        em.raw("tensor", lambda e, rc=rc, oh=oh, m=m, base=base: e.matmul(
                            P1[:, base + oh * 512:base + oh * 512 + 512],
                            bf(FEATS_O + rc * 512 + m * 128, 128),
                            bf(WT_O + rc * 1024 + oh * 512, 512),
                            start=(rc == 0), stop=False))
                for oc in range(DK):
                    em.raw("tensor", lambda e, oc=oc, m=m, base=base, xb=xb: e.matmul(
                        P1[:, base + oc * 128:base + oc * 128 + 128],
                        bf(xb + oc * 512 + m * 128, 128), ident,
                        start=False, stop=False))
                for oh in range(2):
                    def f(e, oh=oh, m=m, base=base):
                        return e.matmul(
                            P1[:, base + oh * 512:base + oh * 512 + 512],
                            bf(FEATS_O + (RC - 1) * 512 + m * 128, 128),
                            bf(WT_O + (RC - 1) * 1024 + oh * 512, 512),
                            start=False, stop=True)
                    if oh == 1:
                        ev["mm_z1", i] = em.inc("tensor", "MM", f)
                    else:
                        em.raw("tensor", lambda e, f=f: f(e))
                emit_ln_head(i, P1[:, base:base + 1024], phase=0)
                if m < 2 or t == NT - 1:
                    emit_ln_tail(i, bf(X1TOK_O + (i % 4) * 1024, 1024), phase=0)

        def emit_tr(t, m):
            i = 4 * t + m
            em.wait("tensor", "VE" if ev.get(("applyv", i)) else "SE",
                    ev["se_apply", i])
            if i >= 1:
                em.wait("tensor", "VE", ev["ve_evac", i - 1])
            for dk in range(DK):
                def f(e, dk=dk, i=i):
                    return e.transpose(ptr[:, dk * 128:dk * 128 + 128],
                                       bf(X1TOK_O + (i % 4) * 1024 + dk * 128, 128),
                                       ident)
                if dk == DK - 1:
                    ev["mm_tr", i] = em.inc("tensor", "MM", f)
                else:
                    em.raw("tensor", lambda e, f=f: f(e))
            em.wait("vector", "MM", ev["mm_tr", i])
            if i == 0:
                em.wait("vector", "MM", ev["mm_swones", DK - 1])
            if m == 0:
                em.wait("vector", "XSS", 16 * t)
            dst = BF[:, X1TB_O:X1TB_O + 4096].rearrange(
                "p (dk t) -> p dk t", dk=DK)[:, :, m * 128:m * 128 + 128]
            srcp = ptr[:, :].rearrange("p (dk t) -> p dk t", dk=DK)
            ev["ve_evac", i] = em.inc(
                "vector", "VE", lambda e, dst=dst, srcp=srcp: e.tensor_copy(dst, srcp))
            if m == 3:
                em.wait("sync", "VE", ev["ve_evac", i])
                em.wait("sync", "XSS", 16 * t)
                ev["xs_st", t] = em.inc(
                    "sync", "XSS",
                    lambda e, t=t: e.dma_start(
                        x1s_h[:, t * 4096:(t + 1) * 4096], bf(X1TB_O, 4096)), 16)
                if t == 0:
                    em.wait("sync", "MM", ev["mm_swones", DK - 1])
                    em.wait("sync", "XSS", ev["xs_st", 0])
                    ev["xs_ld", 0] = em.inc(
                        "sync", "XSL",
                        lambda e: e.dma_start(bf(X1TB_O + 4096, 4096),
                                              x1s_h[:, 0:4096]), 16)

        emit_xsq(0)
        emit_ones_mm(0)
        emit_srow(0)
        emit_mm1(0)
        for dk in range(DK):
            emit_iwsq(dk)
            emit_swones(dk)
        emit_swrow()
        emit_dotw()
        ev["ld_wgT"] = dma("gpsimd", "Lwg", bf(WGT_O, 24576), wgT_h[:, :])
        emit_xsq(1)
        emit_ones_mm(1)
        emit_srow(1)
        emit_mm2(0)
        for t in range(1, NT):
            if t + 1 < NT:
                em.wait("gpsimd", "MM", ev["mm_z1", 4 * (t - 1) + 3])
                ev["ld_xfm", t + 1] = dma(
                    "gpsimd", xfsem(t + 1), xfm_dst(t + 1), xfm_src(t + 1))
                emit_xsq(t + 1)
            if t == 2:
                ev["ld_wuT1"] = dma("gpsimd", "Lwu", bf(WUT1_O, 18432),
                                    wuT_h[:, 0:18432])
                em.wait("gpsimd", "MM", ev["mm_dotw", RC - 1])
                ev["ld_wuT2"] = dma("gpsimd", "Lwu", bf(IWT_O, 6144),
                                    wuT_h[:, 18432:24576])
            emit_mm1(t)
            if t + 1 < NT:
                emit_ones_mm(t + 1)
                emit_srow(t + 1)
            for m in (2, 3):
                emit_ln_tail(4 * (t - 1) + m,
                             bf(X1TOK_O + ((4 * (t - 1) + m) % 4) * 1024, 1024), phase=0)
            emit_mm2(t)
        em.wait("gpsimd", "MM", ev["mm_z1", 4 * NT - 1])
        for j, reg in enumerate(WDT_R):
            ev["ld_wdT", j] = dma("gpsimd", "Lwd", bf(reg, 8192),
                                  wdT_h[:, 8192 * j:8192 * (j + 1)])

        def sg_ap(j):
            return bf(SWROW_O + (j % 2) * 512, 512)

        B_ORDER = [0, 1, 2, 3]
        B_BUF = [1, 0, 1, 0]

        def emit_x1t_load(i):
            em.wait("sync", "XTS", ev["x1t_st", i])
            em.wait("sync", "XTL", 16 * i)
            if i >= 2:
                em.wait("sync", "VE", ev["psfreeB", i - 2])
            ev["x1t_ld", i] = em.inc(
                "sync", "XTL",
                lambda e, i=i: e.dma_start(
                    xres_ap(i), x1t_h[i * 128:(i + 1) * 128, :]), 16)

        def emit_lnB(pi_src, ms):
            t_src = B_ORDER[pi_src]
            for m in ms:
                i = 4 * pi_src + m
                emit_ln(i, P1[:, (m % 2) * 1024:(m % 2) * 1024 + 1024],
                        fa(FOUT_O + (i % 2) * 1024, 1024), phase=1)
                r0 = t_src * TILE + m * 128
                if i == 4 * NT - 1:
                    em.wait("sync", "SE", ev["applyB_lo", i])
                    em.wait("sync", "ST", 16 * i)
                    em.inc("sync", "ST",
                           lambda e, r0=r0, i=i: e.dma_start(
                               out_h[r0:r0 + 128, 0:512],
                               fa(FOUT_O + (i % 2) * 1024, 512)), 16)
                    em.wait("sync", "VE", ev["se_applyB", i])
                    em.inc("sync", "ST",
                           lambda e, r0=r0, i=i: e.dma_start(
                               out_h[r0:r0 + 128, 512:1024],
                               fa(FOUT_O + (i % 2) * 1024 + 512, 512)), 16)
                else:
                    em.wait("sync", "SE", ev["se_applyB", i])
                    em.wait("sync", "ST", 16 * i)
                    em.inc("sync", "ST",
                           lambda e, r0=r0, i=i: e.dma_start(
                               out_h[r0:r0 + 128, :],
                               fa(FOUT_O + (i % 2) * 1024, 1024)), 16)

        def emit_x1_prefetch(pi):
            if pi + 1 < NT and ("xs_ld", B_ORDER[pi + 1]) not in ev:
                tn = B_ORDER[pi + 1]
                em.wait("sync", "XSS", ev["xs_st", tn])
                if B_BUF[pi + 1] == 0:
                    em.wait("sync", "XSS", ev["xs_st", NT - 1])
                em.wait("sync", "XSL", em.c["XSL"])
                if pi >= 1:
                    em.wait("sync", "MM", ev["mm_z2", 4 * (pi - 1) + 3])
                ev["xs_ld", tn] = em.inc(
                    "sync", "XSL",
                    lambda e, tn=tn, pi=pi: e.dma_start(
                        bf(X1TB_O + B_BUF[pi + 1] * 4096, 4096),
                        x1s_h[:, tn * 4096:(tn + 1) * 4096]), 16)

        def bxb1(pi):
            return X1TB_O + B_BUF[pi] * 4096

        def emit_gate(pi, hc, mm_only=False):
            xb1 = bxb1(pi)
            x1_ev = ev.get(("xs_ld", B_ORDER[pi]))
            if True:
                j = HC * pi + hc
                if j == 0:
                    em.wait("tensor", "Lwg", ev["ld_wgT"])
                if hc == 0 and x1_ev is not None:
                    em.wait("tensor", "XSL", x1_ev)
                if j >= 2:
                    em.wait("tensor", "SE", ev["se_silu", j - 2])
                pg = P2[:, (j % 2) * 512:(j % 2) * 512 + 512]
                for dk in range(DK):
                    def f(e, dk=dk, hc=hc, xb1=xb1, pg=pg):
                        return e.matmul(
                            pg, bf(WGT_O + dk * 3072 + hc * 128, 128),
                            bf(xb1 + dk * 512, 512),
                            start=(dk == 0), stop=(dk == DK - 1))
                    if dk == DK - 1:
                        ev["mm_g", j] = em.inc("tensor", "MM", f)
                    else:
                        em.raw("tensor", lambda e, f=f: f(e))
                if not mm_only:
                    emit_silu(pi, hc)

        def emit_silu(pi, hc):
            if True:
                j = HC * pi + hc
                pg = P2[:, (j % 2) * 512:(j % 2) * 512 + 512]
                em.wait("scalar", "MM", ev["mm_g", j])
                if j >= 2:
                    em.wait("scalar", "VE", ev["ve_h", j - 2])
                ev["se_silu", j] = em.inc(
                    "scalar", "SE", lambda e, j=j, pg=pg: e.activation(
                        sg_ap(j), pg, AF.Silu, scale=1.0 / 127.0))

        def emit_up(pi, hc):
            xb1 = bxb1(pi)
            if True:
                j = HC * pi + hc
                if j == 0:
                    em.wait("tensor", "Lwu", ev["ld_wuT2"])
                if pi == 0 and hc == 0:
                    em.wait("tensor", "SE", ev["se_srow", NT - 1])
                if pi == 0 and hc == 1:
                    em.wait("tensor", "VE", ev["ve_evac", 4 * NT - 1])
                if j >= 2:
                    em.wait("tensor", "VE", ev["ve_h", j - 2])
                puj = pu[:, (j % 2) * 512:(j % 2) * 512 + 512]
                for dk in range(DK):
                    wu_off = (WUT1_O + dk * 3072) if dk < 6 else (IWT_O + (dk - 6) * 3072)
                    def f(e, dk=dk, hc=hc, xb1=xb1, puj=puj, wu_off=wu_off):
                        return e.matmul(
                            puj, bf(wu_off + hc * 128, 128),
                            bf(xb1 + dk * 512, 512),
                            start=(dk == 0), stop=(dk == DK - 1))
                    if dk == DK - 1:
                        ev["mm_u", j] = em.inc("tensor", "MM", f)
                    else:
                        em.raw("tensor", lambda e, f=f: f(e))
                em.wait("vector", "MM", ev["mm_u", j])
                em.wait("vector", "SE", ev["se_silu", j])
                if pi == 0 and hc >= 20:
                    em.wait("vector", "XTS",
                            ev["x1t_st", 4 * (NT - 1) + hc - 20])
                if hc == 0 and pi >= 1:
                    em.wait("vector", "MM", ev["mm_z2", 4 * (pi - 1) + 3])
                ev["ve_h", j] = em.inc(
                    "vector", "VE", lambda e, j=j, hc=hc, puj=puj: e.scalar_tensor_tensor(
                        hact_ap(hc, 0, 512), puj, 1.0 / 16129.0, sg_ap(j),
                        OP.mult, OP.mult))

        for pi, t in enumerate(B_ORDER):
            xb1 = bxb1(pi)
            if pi >= 1:
                emit_x1_prefetch(pi)
            if pi == 0:
                for hc in range(4):
                    if hc >= 2:
                        emit_apply_v(4 * (NT - 1) + hc)
                    emit_tr(NT - 1, hc)
                    emit_gate(pi, hc, mm_only=True)
                    if hc < 2:
                        emit_silu(pi, hc)
                for hc in range(4):
                    emit_up(pi, hc)
                    if hc < 2:
                        emit_silu(pi, hc + 2)
                for hc in range(4, HC):
                    emit_gate(pi, hc)
                    emit_up(pi, hc)
            else:
                for hc in range(2, HC):
                    emit_up(pi, hc - 2)
                    emit_gate(pi, hc)
                emit_up(pi, HC - 2)
                emit_up(pi, HC - 1)
            if pi == 0:
                emit_x1_prefetch(pi)
            if pi == 0:
                emit_x1t_load(0)
                emit_x1t_load(1)
                em.wait("tensor", "Lwd", ev["ld_wdT", 2])
            for m in range(4):
                i = 4 * pi + m
                if i >= 2:
                    em.wait("tensor", "VE", ev["psfreeB", i - 2])
                base = (m % 2) * 1024
                for hc in range(HC - 1):
                    if m == 0:
                        em.wait("tensor", "VE", ev["ve_h", HC * pi + hc])
                    reg = WDT_R[hc // 8]
                    for oh in range(2):
                        em.raw("tensor", lambda e, hc=hc, oh=oh, m=m, base=base, reg=reg:
                               e.matmul(
                                   P1[:, base + oh * 512:base + oh * 512 + 512],
                                   hact_ap(hc, m * 128, 128),
                                   bf(reg + (hc % 8) * 1024 + oh * 512, 512),
                                   start=(hc == 0), stop=False))
                hcl = HC - 1
                regl = WDT_R[hcl // 8]
                if m == 0:
                    em.wait("tensor", "VE", ev["ve_h", HC * pi + hcl])
                for oh in range(2):
                    def f(e, oh=oh, m=m, base=base, hcl=hcl, regl=regl):
                        return e.matmul(
                            P1[:, base + oh * 512:base + oh * 512 + 512],
                            hact_ap(hcl, m * 128, 128),
                            bf(regl + (hcl % 8) * 1024 + oh * 512, 512),
                            start=False, stop=True)
                    if oh == 1:
                        ev["mm_z2", i] = em.inc("tensor", "MM", f)
                    else:
                        em.raw("tensor", lambda e, f=f: f(e))
                emit_lnB(pi, (m,))
                if i + 2 <= 4 * NT - 1:
                    emit_x1t_load(i + 2)
                if m < 2 and pi + 1 < NT:
                    emit_gate(pi + 1, m)

        em.wait("sync", "ST", 16 * (4 * NT + 1))

        with nc.Block() as block:
            @block.sync
            def _(e):
                for f in em.q["sync"]:
                    f(e)

            @block.gpsimd
            def _(e):
                for f in em.q["gpsimd"]:
                    f(e)

            @block.vector
            def _(e):
                for f in em.q["vector"]:
                    f(e)

            @block.scalar
            def _(e):
                for f in em.q["scalar"]:
                    f(e)

            @block.tensor
            def _(e):
                for f in em.q["tensor"]:
                    f(e)

    return nc


_NC_CACHE = None


def _get_nc():
    global _NC_CACHE
    if _NC_CACHE is None:
        _NC_CACHE = build_nc()
    return _NC_CACHE


def kernel(L0_attn_x, L0_attn_input_weights, L0_attn_xis, L0_attn_proj_matrix,
           L0_ffn_w_gate, L0_ffn_w_up, L0_ffn_w_down):
    x = np.asarray(L0_attn_x, dtype=np.float32)
    iw = np.asarray(L0_attn_input_weights, dtype=np.float32)
    xis = np.asarray(L0_attn_xis, dtype=np.float32)
    proj = np.asarray(L0_attn_proj_matrix, dtype=np.float32)
    wg = np.asarray(L0_ffn_w_gate, dtype=np.int32)
    wu = np.asarray(L0_ffn_w_up, dtype=np.int32)
    wd = np.asarray(L0_ffn_w_down, dtype=np.int32)

    def pack(mT, nchunk):
        f = mT.shape[1]
        return np.ascontiguousarray(
            mT.reshape(nchunk, 128, f).transpose(1, 0, 2).reshape(128, nchunk * f))

    xr = x.reshape(16384, D)
    xfmT = xr.T
    projT = np.ascontiguousarray(
        proj.T.reshape(DK, 128, 2, 512).transpose(1, 2, 0, 3).reshape(128, 8192))
    iwT = pack(np.ascontiguousarray(iw.T), DK)
    wgT = pack(np.ascontiguousarray(wg.T.astype(np.int8)), DK)
    wuT = pack(np.ascontiguousarray(wu.T.astype(np.int8)), DK)
    wdT = pack(np.ascontiguousarray(wd.T.astype(np.int8)), HC)
    xisr = np.ascontiguousarray(xis.reshape(1, D))
    xisc = np.ascontiguousarray(xis.reshape(DK, 128).T)

    in_maps = []
    for c in range(NCORES):
        xfc = xfmT[:, c * T:(c + 1) * T].reshape(DK, 128, NT, TILE)
        xfc = np.ascontiguousarray(
            xfc.transpose(1, 2, 0, 3).reshape(128, NT * 4096))
        in_maps.append(dict(
            xfm=xfc,
            projT=projT, iwT=iwT, wgT=wgT, wuT=wuT, wdT=wdT,
            xisr=xisr, xisc=xisc))

    nc = _get_nc()
    res = run_bass_kernel_spmd(nc, in_maps, core_ids=list(range(NCORES)))
    out = np.concatenate([res.results[c]["out"] for c in range(NCORES)], axis=0)
    return out.reshape(2, 8192, D)
